# version 01 (final)
# baseline (speedup 1.0000x reference)
import contextlib
import numpy as np
import ml_dtypes
import concourse.bass as bass
import concourse.mybir as mybir
from concourse.bass import AP
from concourse.bass_utils import run_bass_kernel_spmd

F32 = mybir.dt.float32
BF16 = mybir.dt.bfloat16
I32 = mybir.dt.int32
ALU = mybir.AluOpType
ACT = mybir.ActivationFunctionType
AXL = mybir.AxisListType

B_, T_, D_ = 4, 4096, 1024
DE = 2048
RK = 64
HID = 4096
NEXP = 8
TOK = 2048
XROWS = 2176
NXT = 17
SD = 1024
SCH = 8
CW = 512

DEBUG = False
DEBUG_HEAVY = False
import os as _os
THROTTLE_DL = _os.environ.get("THR_DL", "0") == "1"
THROTTLE_Q = True
ENGS = ("sync", "tensor", "vector", "scalar", "gpsimd")
RG_PAIRS = [[0, 1], [2, 3], [4, 5], [6, 7]]

_BUILT = {}


def build_nc():
    nc = bass.Bass("TRN2", target_bir_lowering=False)

    def din(name, shape, dt=F32):
        return nc.dram_tensor(name, shape, dt, kind="ExternalInput").ap()

    x_in = din("x_in", [XROWS, D_])
    w_inT = din("w_inT", [32, 128, 8, 128])
    convw = din("convw", [128, 48])
    convb = din("convb", [128, 16])
    dvec = din("dvec", [128, 16])
    w_xprojT = din("w_xprojT", [DE, 96])
    w_dtT = din("w_dtT", [RK, SD])
    b_dt = din("b_dt", [128, 8])
    a_pack = din("a_pack", [128, 128])
    w_outT = din("w_outT", [8, 128, 16, 128])
    wfc_s = din("wfc_s", [32, 128, 8, 128])
    bfc_s_in = din("bfc_s", [128, 32])
    wproj_s = din("wproj_s", [8, 128, 32, 128])
    bproj_s_in = din("bproj_s", [128, 8])
    wr_in = din("wr", [D_, NEXP])
    wfc_in = din("wfc", [NEXP, 32, 128, 8, 128])
    bfc_in = din("bfc", [128, NEXP * 32])
    wproj_in = din("wproj", [NEXP, 8, 128, 32, 128])
    bproj_in = din("bproj", [NEXP, D_])
    ident_in = din("ident", [128, 128])
    hmask_in = din("hmask", [128, 2])

    out_d = nc.dram_tensor("out", [TOK, D_], F32, kind="ExternalOutput").ap()

    def dint(name, shape, dt):
        return nc.dram_tensor(name, shape, dt, kind="Internal").ap()

    uc_dram = dint("uc_dram", [DE, TOK], BF16)
    z_dram = dint("z_dram", [DE, TOK], BF16)
    xn_dram = dint("xn_dram", [D_, TOK], BF16)
    gate_dram = dint("gate_dram", [NEXP, TOK], F32)
    cc1_out = dint("cc1_out", [2 * SD, TOK], BF16)
    cc2_in = dint("cc2_in", [96, TOK], F32)
    cc2_out = dint("cc2_out", [192, TOK], F32)
    y_all = dint("y_all", [SD, T_], BF16)
    cc3_out = dint("cc3_out", [2 * SD, T_], BF16)

    dbg = {}
    if DEBUG:
        for nm, shape, dt in [
            ("dbg_xn", [D_, TOK], BF16), ("dbg_uc", [DE, TOK], BF16),
            ("dbg_z", [DE, TOK], BF16), ("dbg_bcdt", [192, TOK], F32),
            ("dbg_gate", [NEXP, TOK], F32), ("dbg_yall", [2 * SD, T_], BF16),
            ("dbg_acc", [D_, TOK], F32),
            ("dbg_lgt", [128, 128], F32), ("dbg_gtok", [128, 128], F32),
            ("dbg_gfm", [NEXP, TOK], F32),
            ("dbg_delta", [SD, T_], F32), ("dbg_q", [SD, T_], BF16),
            ("dbg_xnst", [128, 8 * 512], BF16), ("dbg_hact", [128, 32 * 512], BF16),
            ("dbg_tes", [128, 512], F32), ("dbg_wfc", [128, 8 * 128], BF16),
            ("dbg_grep", [128, 512], F32),
        ]:
            dbg[nm] = nc.dram_tensor(nm, shape, dt, kind="ExternalOutput").ap()

    st = contextlib.ExitStack()
    with st:
        def sbuf(name, shape, dt):
            return st.enter_context(nc.sbuf_tensor(name, shape, dt))

        s_convw = sbuf("s_convw", [128, 48], F32)
        s_convb = sbuf("s_convb", [128, 16], F32)
        s_dvec = sbuf("s_dvec", [128, 16], F32)
        s_bdt = sbuf("s_bdt", [128, 8], F32)
        s_alog = sbuf("s_alog", [128, 128], F32)
        s_aneg = sbuf("s_aneg", [128, 128], F32)
        s_ident = sbuf("s_ident", [128, 128], F32)
        s_hmask = sbuf("s_hmask", [128, 2], F32)
        s_bfcs = sbuf("s_bfcs", [128, 32], F32)
        s_bprs = sbuf("s_bprs", [128, 8], F32)
        s_bfc = sbuf("s_bfc", [128, NEXP * 32], F32)
        s_bproj = sbuf("s_bproj", [8, D_], BF16)
        s_wdtT = sbuf("s_wdtT", [RK, SD], BF16)
        s_wxp = sbuf("s_wxp", [128, 16 * 96], BF16)
        s_wr = sbuf("s_wr", [128, 64], BF16)
        s_gate = sbuf("s_gate", [8, TOK], F32)
        s_gateb = sbuf("s_gateb", [8, TOK], BF16)
        s_carry = sbuf("s_carry", [128, 128], F32)
        s_lnst = sbuf("s_lnst", [128, 16], F32)
        s_sqs = sbuf("s_sqs", [128, 1024], F32)
        s_gsc = sbuf("s_gsc", [128, 8], F32)
        s_geq1 = sbuf("s_geq1", [128, 8], F32)
        s_geq2 = sbuf("s_geq2", [128, 8], F32)
        s_glg2 = sbuf("s_glg2", [128, 8], F32)
        s_gm1 = sbuf("s_gm1", [128, 16], F32)
        s_gm2 = sbuf("s_gm2", [128, 16], F32)
        s_gd = sbuf("s_gd", [128, 16], F32)
        s_gt_ = sbuf("s_gt_", [128, 16], F32)
        s_gw2 = sbuf("s_gw2", [128, 16], F32)
        s_geqA = sbuf("s_geqA", [128, 128], F32)
        s_geqB = sbuf("s_geqB", [128, 128], F32)
        s_glgA = sbuf("s_glgA", [128, 128], F32)
        s_cst = sbuf("s_cst", [128, 4], F32)

        AR1 = sbuf("AR1", [128, 18304], F32)
        AR2 = sbuf("AR2", [128, 27552], F32)

        class Arena:
            def __init__(self, base, off=0):
                self.base, self.off = base, off

            def f32(self, cols):
                a = self.base[:, self.off:self.off + cols]
                self.off += cols
                assert self.off <= self.base.shape[1], f"arena overflow {self.off}"
                return a

            def bf16(self, cols):
                assert cols % 2 == 0
                a = self.base[:, self.off:self.off + cols // 2].bitcast(BF16)
                self.off += cols // 2
                assert self.off <= self.base.shape[1], f"arena overflow {self.off}"
                return a

        pb = [st.enter_context(nc.psum_tensor(f"pb{i}", [128, 512], F32)) for i in range(8)]

        SEM, CNT = {}, {}
        SQ = {e: [] for e in ENGS}

        def sem(name):
            if name not in SEM:
                SEM[name] = st.enter_context(nc.semaphore(name))
                CNT[name] = 0
            return SEM[name]

        def W(q, s, v=None):
            if s is None:
                return
            if isinstance(s, tuple):
                s, v = s
            sh = sem(s)
            vv = CNT[s] if v is None else v
            assert vv <= CNT[s], f"future wait {s}: {vv} > {CNT[s]}"
            if vv <= 0:
                return
            SQ[q].append(lambda e, _s=sh, _v=vv: e.wait_ge(_s, _v))

        def OP(q, fn, inc=None, by=1):
            if inc is not None:
                sh = sem(inc)
                CNT[inc] += by
                SQ[q].append(lambda e, fn=fn, _s=sh, _b=by: fn(e).then_inc(_s, _b))
                return (inc, CNT[inc])
            SQ[q].append(lambda e, fn=fn: fn(e))
            return None

        def SELFW(q, fn):
            tok = OP(q, fn, inc="sw_" + q)
            W(q, tok)
            return tok

        GQCAP = int(_os.environ.get("GQCAP", "10"))
        gq_toks = []

        def DMA(q, out, in_, inc=None, **kw):
            def f(e, o=out, i=in_, kw=dict(kw)):
                return e.dma_start(out=o, in_=i, **kw)
            if q == "gpsimd":
                if inc is None:
                    inc = "gq"
                if len(gq_toks) >= GQCAP:
                    W(q, gq_toks[len(gq_toks) - GQCAP])
                tok = OP(q, f, inc=inc, by=16)
                gq_toks.append(tok)
                return tok
            return OP(q, f, inc=inc, by=16)

        class RB:
            def __init__(self, aps):
                self.aps = aps
                self.toks = [None] * len(aps)
                self.i = 0

            def acq(self, q):
                j = self.i % len(self.aps)
                if self.toks[j] is not None:
                    for t in self.toks[j]:
                        W(q, t)
                return self.aps[j]

            def ret(self, tok):
                if tok is None or isinstance(tok, tuple):
                    tok = [tok] if tok is not None else []
                self.toks[self.i % len(self.aps)] = tok
                self.i += 1

        def rearr_w(src_ap):
            return src_ap.rearrange("(kt p) m -> p kt m", p=128)

        def v3(dst_ap, kt):
            return dst_ap.rearrange("p (kt m) -> p kt m", kt=kt)

        DMA("sync", s_convw[:], convw, inc="k")
        DMA("sync", s_convb[:], convb, inc="k")
        DMA("sync", s_dvec[:], dvec, inc="k")
        DMA("sync", s_bdt[:], b_dt, inc="k")
        DMA("sync", s_alog[:], a_pack, inc="k")
        DMA("sync", s_ident[:], ident_in, inc="k")
        DMA("sync", s_hmask[:], hmask_in, inc="k")
        DMA("sync", s_bfcs[:], bfc_s_in, inc="k")
        DMA("sync", s_bprs[:], bproj_s_in, inc="k")
        DMA("sync", s_bfc[:], bfc_in, inc="k")
        DMA("gpsimd", s_bproj[:8, :], bproj_in, inc="k")
        DMA("gpsimd", s_wdtT[:RK, :], w_dtT, inc="k")
        DMA("gpsimd", v3(s_wxp[:], 16), w_xprojT.rearrange("(dt p) m -> p dt m", p=128), inc="k")
        DMA("gpsimd", v3(s_wr[:], 8), rearr_w(wr_in), inc="k")
        kload = ("k", CNT["k"])

        OP("vector", lambda e: e.memset(s_cst[:, 0:1], 0.5), inc="cst")
        OP("vector", lambda e: e.memset(s_cst[:, 1:2], -0.5), inc="cst")
        cst_tok = ("cst", CNT["cst"])
        W("scalar", cst_tok)

        arA = Arena(AR1)
        arA2 = Arena(AR2)
        xn_sb = arA2.bf16(8 * XROWS)
        conv_f32 = arA.f32(2048)
        bc_sb = arA.f32(2048)
        lgt_sb = arA.f32(16 * 8)
        gt_sb = arA.f32(16 * 8)
        x_rb = RB([arA.f32(1024) for _ in range(2)])
        xt_rb = RB([arA.bf16(1024) for _ in range(2)])
        u_rb = RB([arA.f32(2052) for _ in range(2)])
        z_rb = RB([arA.bf16(2048) for _ in range(2)])
        uc_rb = RB([arA.bf16(2048) for _ in range(3)])
        wi_rb = RB([arA.bf16(8 * 128) for _ in range(3)])

        W("sync", kload)
        for ti in range(NXT):
            xs = x_rb.acq("sync")
            xltok = DMA("sync", xs[:], x_in[ti * 128:(ti + 1) * 128, :], inc="xl")
            c0 = 8 * (ti % 2)
            W("vector", xltok)
            OP("vector", lambda e, t=xs, c0=c0: e.reduce_sum(s_lnst[:, c0:c0 + 1], t[:], AXL.X), inc="lnv")
            W("scalar", xltok)
            OP("scalar", lambda e, t=xs, c0=c0: e.activation(s_sqs[:], t[:], ACT.Square, accum_out=s_lnst[:, c0 + 1:c0 + 2]), inc="lns")
            W("vector", "lns")
            SELFW("vector", lambda e, c0=c0: e.tensor_scalar(s_lnst[:, c0 + 2:c0 + 3], s_lnst[:, c0:c0 + 1], 1.0 / 1024.0, None, ALU.mult))
            SELFW("vector", lambda e, c0=c0: e.tensor_tensor(s_lnst[:, c0 + 3:c0 + 4], s_lnst[:, c0 + 2:c0 + 3], s_lnst[:, c0 + 2:c0 + 3], ALU.mult))
            SELFW("vector", lambda e, c0=c0: e.tensor_scalar(s_lnst[:, c0 + 4:c0 + 5], s_lnst[:, c0 + 1:c0 + 2], 1.0 / 1024.0, None, ALU.mult))
            OP("vector", lambda e, c0=c0: e.tensor_tensor(s_lnst[:, c0 + 4:c0 + 5], s_lnst[:, c0 + 4:c0 + 5], s_lnst[:, c0 + 3:c0 + 4], ALU.subtract), inc="lnv")
            OP("vector", lambda e, c0=c0: e.tensor_scalar(s_lnst[:, c0 + 4:c0 + 5], s_lnst[:, c0 + 4:c0 + 5], 1e-5, None, ALU.add), inc="lnv")
            W("scalar", "lnv")
            SELFW("scalar", lambda e, c0=c0: e.activation(s_lnst[:, c0 + 5:c0 + 6], s_lnst[:, c0 + 4:c0 + 5], ACT.Ln))
            OP("scalar", lambda e, c0=c0: e.activation(s_lnst[:, c0 + 6:c0 + 7], s_lnst[:, c0 + 5:c0 + 6], ACT.Exp, scale=s_cst[:, 1:2]), inc="lns")
            W("vector", "lns")
            OP("vector", lambda e, c0=c0: e.tensor_scalar(s_lnst[:, c0 + 7:c0 + 8], s_lnst[:, c0 + 2:c0 + 3], s_lnst[:, c0 + 6:c0 + 7], -1.0, ALU.mult, ALU.mult), inc="lnv")
            xt = xt_rb.acq("scalar")
            W("scalar", "lnv")
            cptok = OP("scalar", lambda e, t=xs, o=xt, c0=c0: e.activation(o[:], t[:], ACT.Identity, bias=s_lnst[:, c0 + 7:c0 + 8], scale=s_lnst[:, c0 + 6:c0 + 7]), inc="lns")
            x_rb.ret(cptok)
            W("sync", cptok)
            ttok = None
            for cb in range(8):
                ttok = DMA("sync", xn_sb[:, cb * XROWS + ti * 128: cb * XROWS + (ti + 1) * 128],
                           xt[:, cb * 128:(cb + 1) * 128], inc="xnt", transpose=True)
            xt_rb.ret(ttok)
        xnt_all = ("xnt", CNT["xnt"])

        W("gpsimd", xnt_all)
        for kt in range(8):
            DMA("gpsimd", xn_dram[kt * 128:(kt + 1) * 128, :],
                xn_sb[:, kt * XROWS + 1: kt * XROWS + 1 + TOK], inc="xnd")

        ngrp_u = [(0, 512), (512, 512), (1024, 512), (1536, 512), (2048, 2)]
        ubank = RB([0, 1])

        class BankRB(RB):
            pass
        ubank = BankRB([0, 1])
        for mt in range(16):
            ws = wi_rb.acq("gpsimd")
            wtok = DMA("gpsimd", v3(ws[:], 8), w_inT[mt], inc="wi")
            W("tensor", wtok)
            W("tensor", xnt_all)
            us = u_rb.acq("scalar")
            mmtok = None
            etok = None
            for (goff, gn) in ngrp_u:
                bk = ubank.acq("tensor")
                bank = pb[bk]
                for kt in range(8):
                    def mmfn(e, b=bank, w=ws, kt=kt, goff=goff, gn=gn):
                        return e.matmul(b[:, 0:gn], w[:, kt * 128:(kt + 1) * 128],
                                        xn_sb[:, kt * XROWS + goff: kt * XROWS + goff + gn],
                                        start=(kt == 0), stop=(kt == 7))
                    mmtok = OP("tensor", mmfn, inc=("peu" if kt == 7 else None))
                W("scalar", mmtok)
                etok = OP("scalar", lambda e, b=bank, u=us, goff=goff, gn=gn:
                          e.copy(u[:, goff:goff + gn], b[:, 0:gn]), inc="sue")
                ubank.ret(etok)
            wi_rb.ret(mmtok)
            uctile = uc_rb.acq("scalar")
            W("vector", etok)
            OP("vector", lambda e, u=us, mt=mt: e.tensor_scalar(
                conv_f32[:], u[:, 1:2049], s_convw[:, mt * 3 + 1: mt * 3 + 2], s_convb[:, mt:mt + 1], ALU.mult, ALU.add))
            OP("vector", lambda e, u=us, mt=mt: e.scalar_tensor_tensor(
                conv_f32[:], u[:, 0:2048], s_convw[:, mt * 3 + 0: mt * 3 + 1], conv_f32[:], ALU.mult, ALU.add))
            cvtok = OP("vector", lambda e, u=us, mt=mt: e.scalar_tensor_tensor(
                conv_f32[:], u[:, 2:2050], s_convw[:, mt * 3 + 2: mt * 3 + 3], conv_f32[:], ALU.mult, ALU.add), inc="vcv")
            u_rb.ret(cvtok)
            W("scalar", cvtok)
            sitok = OP("scalar", lambda e, o=uctile: e.activation(o[:], conv_f32[:], ACT.Silu), inc="ssi")
            W("tensor", sitok)
            pjtok = None
            for g4 in range(4):
                def pjfn(e, mt=mt, g4=g4, cv=uctile):
                    return e.matmul(pb[4 + g4][0:96, :], s_wxp[:, mt * 96:(mt + 1) * 96],
                                    cv[:, g4 * 512:(g4 + 1) * 512],
                                    start=(mt == 0), stop=(mt == 15))
                pjtok = OP("tensor", pjfn, inc=("pjA" if g4 == 3 else None))
            W("gpsimd", sitok)
            udtok = DMA("gpsimd", uc_dram[mt * 128:(mt + 1) * 128, :], uctile[:], inc="ucd")
            uc_rb.ret([pjtok, udtok])
        pjA_all = ("pjA", CNT["pjA"])

        zbank = BankRB([0, 1])
        for mt in range(16):
            ws = wi_rb.acq("gpsimd")
            wtok = DMA("gpsimd", v3(ws[:], 8), w_inT[16 + mt], inc="wi")
            W("tensor", wtok)
            zs = z_rb.acq("scalar")
            mmtok = None
            etok = None
            for g4 in range(4):
                bk = zbank.acq("tensor")
                bank = pb[bk]
                for kt in range(8):
                    def mmfn(e, b=bank, w=ws, kt=kt, g4=g4):
                        return e.matmul(b[:], w[:, kt * 128:(kt + 1) * 128],
                                        xn_sb[:, kt * XROWS + 1 + g4 * 512: kt * XROWS + 1 + (g4 + 1) * 512],
                                        start=(kt == 0), stop=(kt == 7))
                    mmtok = OP("tensor", mmfn, inc=("peu" if kt == 7 else None))
                W("scalar", mmtok)
                etok = OP("scalar", lambda e, b=bank, z=zs, g4=g4:
                          e.copy(z[:, g4 * 512:(g4 + 1) * 512], b[:]), inc="sze")
                zbank.ret(etok)
            wi_rb.ret(mmtok)
            W("gpsimd", etok)
            ztok = DMA("gpsimd", z_dram[mt * 128:(mt + 1) * 128, :], zs[:], inc="zd")
            z_rb.ret(ztok)

        W("vector", pjA_all)
        for g4 in range(4):
            OP("vector", lambda e, g4=g4: e.tensor_copy(bc_sb[0:96, g4 * 512:(g4 + 1) * 512], pb[4 + g4][0:96, :]), inc="bce")
        W("gpsimd", "bce")
        DMA("gpsimd", cc2_in, bc_sb[0:96, :], inc="bcd")

        lgf = conv_f32
        for g4 in range(4):
            if g4 == 0:
                W("tensor", "bce")
            rttok = None
            for kt in range(8):
                def rtfn(e, kt=kt, g4=g4):
                    return e.matmul(pb[4][0:8, :], s_wr[:, kt * 8:(kt + 1) * 8],
                                    xn_sb[:, kt * XROWS + 1 + g4 * 512: kt * XROWS + 1 + (g4 + 1) * 512],
                                    start=(kt == 0), stop=(kt == 7))
                rttok = OP("tensor", rtfn, inc=("rt" if kt == 7 else None))
            W("vector", rttok)
            lftok = OP("vector", lambda e: e.tensor_copy(lgf[0:8, 0:512], pb[4][0:8, :]), inc="lgf")
            W("tensor", lftok)
            for q in range(4):
                def trfn(e, q=q):
                    return e.transpose(pb[5][:, q * 8:(q + 1) * 8], lgf[0:8, q * 128:(q + 1) * 128], s_ident[0:8, 0:8])
                OP("tensor", trfn, inc=("rtt" if q == 3 else None))
            W("vector", "rtt")
            lgvtok = None
            for q in range(4):
                tt = g4 * 4 + q
                lgvtok = OP("vector", lambda e, q=q, tt=tt: e.tensor_copy(lgt_sb[:, tt * 8:(tt + 1) * 8], pb[5][:, q * 8:(q + 1) * 8]), inc="lgv")
            W("tensor", lgvtok)
        lgv_all = ("lgv", CNT["lgv"])
        W("vector", lgv_all)
        lgv3 = lgt_sb[:].rearrange("p (a b) -> p a b", a=16)
        gt3 = gt_sb[:].rearrange("p (a b) -> p a b", a=16)
        def bc16(t2d):
            return AP(t2d.tensor, t2d.offset, [[t2d.ap[0][0], 128], [1, 16], [0, 8]])
        SELFW("vector", lambda e: e.tensor_reduce(s_gm1[:], lgv3, AXL.X, ALU.max))
        SELFW("vector", lambda e: e.tensor_tensor(s_geqA[:].rearrange("p (a b) -> p a b", a=16), lgv3, bc16(s_gm1[:]), ALU.is_ge))
        SELFW("vector", lambda e: e.scalar_tensor_tensor(s_glgA[:], s_geqA[:], -1e30, lgt_sb[:], ALU.mult, ALU.add))
        SELFW("vector", lambda e: e.tensor_reduce(s_gm2[:], s_glgA[:].rearrange("p (a b) -> p a b", a=16), AXL.X, ALU.max))
        dtok = OP("vector", lambda e: e.tensor_tensor(s_gd[:], s_gm1[:], s_gm2[:], ALU.subtract), inc="gv")
        W("scalar", dtok)
        ttok = OP("scalar", lambda e: e.activation(s_gt_[:], s_gd[:], ACT.Tanh, scale=s_cst[:, 0:1]), inc="gsg")
        W("vector", ttok)
        SELFW("vector", lambda e: e.tensor_scalar(s_gw2[:], s_gt_[:], -0.5, 0.5, ALU.mult, ALU.add))
        SELFW("vector", lambda e: e.tensor_tensor(s_geqB[:].rearrange("p (a b) -> p a b", a=16), lgv3, bc16(s_gm2[:]), ALU.is_ge))
        SELFW("vector", lambda e: e.tensor_tensor(gt3, s_geqA[:].rearrange("p (a b) -> p a b", a=16), bc16(s_gt_[:]), ALU.mult))
        SELFW("vector", lambda e: e.tensor_tensor(s_glgA[:].rearrange("p (a b) -> p a b", a=16), s_geqB[:].rearrange("p (a b) -> p a b", a=16), bc16(s_gw2[:]), ALU.mult))
        OP("vector", lambda e: e.tensor_tensor(gt_sb[:], s_glgA[:], gt_sb[:], ALU.add), inc="gtok")
        W("tensor", "gtok")
        for g4 in range(4):
            for q in range(4):
                tt = g4 * 4 + q
                def tgfn(e, q=q, tt=tt):
                    return e.transpose(pb[6][0:8, q * 128:(q + 1) * 128], gt_sb[:, tt * 8:(tt + 1) * 8], s_ident[:, 0:128])
                OP("tensor", tgfn, inc=("tg" if q == 3 else None))
            W("vector", "tg")
            gfmtok = OP("vector", lambda e, g4=g4: e.tensor_copy(s_gate[0:8, g4 * 512:(g4 + 1) * 512], pb[6][0:8, :]), inc="gfm")
            W("tensor", gfmtok)
        W("vector", "gfm")
        OP("vector", lambda e: e.tensor_copy(s_gateb[0:8, :], s_gate[0:8, :]), inc="gfm")
        gfm_all = ("gfm", CNT["gfm"])
        tg_all = ("tg", CNT["tg"])
        W("gpsimd", gfm_all)
        DMA("gpsimd", gate_dram, s_gate[0:8, :], inc="gd")

        for s_ in ("ucd", "bcd", "zd", "xnd", "gd", "xl", "xnt", "wi", "k"):
            if CNT.get(s_, 0) > 0:
                W("gpsimd", (s_, CNT[s_]))
        OP("gpsimd", lambda e: e.collective_compute(
            "AllGather", ALU.bypass, replica_groups=RG_PAIRS,
            ins=[uc_dram[SD:2 * SD, :].opt()], outs=[cc1_out.opt()]), inc="cc1")
        OP("gpsimd", lambda e: e.collective_compute(
            "AllGather", ALU.bypass, replica_groups=RG_PAIRS,
            ins=[cc2_in.opt()], outs=[cc2_out.opt()]), inc="cc2")
        cc1tok = ("cc1", CNT["cc1"])
        cc2tok = ("cc2", CNT["cc2"])
        W("gpsimd", cc1tok)
        W("gpsimd", cc2tok)

        OP("scalar", lambda e: e.activation(s_aneg[:], s_alog[:], ACT.Exp), inc="an")
        W("vector", "an")
        OP("vector", lambda e: e.tensor_scalar(s_aneg[:], s_aneg[:], -1.0, None, ALU.mult), inc="anv")
        OP("vector", lambda e: e.memset(s_carry[:], 0.0), inc="anv")
        anv_tok = ("anv", CNT["anv"])

        NG = 2
        arB = Arena(AR1)
        dt_rb = RB([arB.bf16(512) for _ in range(2)])
        dlt_rb = RB([arB.f32(512) for _ in range(2)])
        e1_st = arB.f32(512)
        dA_rb = RB([arB.f32(NG * 512) for _ in range(2)])
        q_st = arB.bf16(512)
        g_rb = RB([arB.bf16(NG * 512) for _ in range(2)])
        h_b = arB.f32(NG * 512)
        hc_b = arB.f32(NG * 512)
        yac = arB.f32(512)
        ytmp = arB.f32(512)
        yst = arB.bf16(8 * 512)
        ucbl = arB.f32(512)
        br_rb = RB([arB.bf16(NG * 512) for _ in range(2)])
        cr_rb = RB([arB.bf16(NG * 512) for _ in range(2)])
        ucg_rb = RB([arB.bf16(2 * 512) for _ in range(2)])

        arC = Arena(AR2)
        acc_sb = arC.f32(8 * TOK)
        wpr_rb = RB([arC.bf16(32 * 128) for _ in range(2)])
        xn_rb = RB([arC.bf16(8 * 512) for _ in range(2)])
        assert arC.off <= AR2.shape[1], f"arC {arC.off}"

        arC1 = Arena(AR2, off=8 * TOK)
        ud_rb = RB([arC1.bf16(512) for _ in range(2)])
        zd_rb = RB([arC1.bf16(512) for _ in range(2)])
        yp_rb = RB([arC1.bf16(2 * 512) for _ in range(2)])
        ybl = arC1.f32(512)
        yf_ch = arC1.bf16(16 * 512)
        y1_sc = arC1.f32(512)
        th_sc = arC1.f32(512)
        sv_sc = arC1.f32(512)
        wo_rb = RB([arC1.bf16(16 * 128) for _ in range(2)])
        assert arC1.off <= AR2.shape[1], f"arC1 {arC1.off}"

        arC4 = Arena(AR1)
        xr_rb = RB([arC4.f32(1024) for _ in range(2)])
        ot_rb = RB([arC4.f32(1024) for _ in range(2)])
        arCx = Arena(AR1, off=arC4.off)
        hact = arCx.bf16(32 * 512)
        wfc_rb = RB([arCx.bf16(8 * 128) for _ in range(3)])
        gr_rb = RB([arCx.f32(512) for _ in range(2)])
        tmp_ev = arCx.f32(512)
        assert arCx.off <= AR1.shape[1], f"arCx {arCx.off}"

        y_prev_dma = [None]

        def emit_scan_chunk(c):
            half, lc = c // 4, c % 4
            cs = lc * 512
            dts = dt_rb.acq("gpsimd")
            if c == 0:
                W("gpsimd", cc2tok)
                W("gpsimd", tg_all)
                W("gpsimd", gfm_all)
            dtt = DMA("gpsimd", dts[0:64, :], cc2_out[half * 96 + 32: half * 96 + 96, cs:cs + 512], inc="sdt")
            if c == 0:
                W("vector", anv_tok)
            if y_prev_dma[0] is not None:
                W("vector", y_prev_dma[0])
            dmm_tok = None
            for g in range(8):
                us = ucg_rb.acq("gpsimd")
                if c == 0 and g == 0:
                    W("gpsimd", cc1tok)
                if half == 0:
                    cand0 = uc_dram[g * 128:(g + 1) * 128, cs:cs + 512]
                    cand1 = cc1_out[g * 128:(g + 1) * 128, cs:cs + 512]
                else:
                    cand0 = cc1_out[SD + g * 128: SD + (g + 1) * 128, cs:cs + 512]
                    cand1 = uc_dram[g * 128:(g + 1) * 128, cs:cs + 512]
                DMA("gpsimd", us[:, 0:512], cand0, inc="sug")
                utok = DMA("gpsimd", us[:, 512:1024], cand1, inc="sug")
                W("tensor", dtt)
                if CNT.get("sdl", 0) > 0:
                    W("tensor", "sdl")
                if c == 0 and g == 0:
                    W("tensor", anv_tok)
                dmm_tok = OP("tensor", lambda e, d=dts, g=g: e.matmul(
                    pb[7][:], s_wdtT[:, g * 128:(g + 1) * 128], d[0:64, :],
                    start=True, stop=True), inc="sdm")
                dl = dlt_rb.acq("scalar")
                W("scalar", dmm_tok)
                SELFW("scalar", lambda e, g=g: e.activation(e1_st[:], pb[7][:], ACT.Exp, bias=s_bdt[:, g:g + 1]))
                dltok = OP("scalar", lambda e, o=dl: e.activation(o[:], e1_st[:], ACT.Ln, bias=1.0), inc="sdl")
                W("scalar", dltok)
                if THROTTLE_DL:
                    W("gpsimd", dltok)
                if DEBUG and DEBUG_HEAVY:
                    DMA("gpsimd", dbg["dbg_delta"][g * 128:(g + 1) * 128, c * 512:(c + 1) * 512], dl[:], inc="dbgs")
                W("vector", dltok)
                W("vector", utok)
                OP("vector", lambda e, u=us: e.tensor_scalar(ucbl[:], u[:, 0:512], s_hmask[:, 0:1], None, ALU.mult))
                OP("vector", lambda e, u=us: e.scalar_tensor_tensor(ucbl[:], u[:, 512:1024], s_hmask[:, 1:2], ucbl[:], ALU.mult, ALU.add))
                qtok = OP("vector", lambda e, d=dl: e.tensor_tensor(q_st[:], d[:], ucbl[:], ALU.mult), inc="svq")
                if THROTTLE_Q:
                    W("gpsimd", qtok)
                if DEBUG and DEBUG_HEAVY:
                    DMA("gpsimd", dbg["dbg_q"][g * 128:(g + 1) * 128, c * 512:(c + 1) * 512], q_st[:], inc="dbgs")
                ucg_rb.ret(qtok)
                dlt_rb.ret(qtok)
                for grp in range(16 // NG):
                    n0 = g * 16 + grp * NG
                    brs = br_rb.acq("gpsimd")
                    soff = (half * 96 + grp * NG) * TOK + cs
                    btok = DMA("gpsimd", brs[:].rearrange("p (a b) -> p a b", a=NG),
                               AP(cc2_out.tensor, soff, [[0, 128], [TOK, NG], [1, 512]]), inc="srep")
                    crs = cr_rb.acq("gpsimd")
                    soff = (half * 96 + 16 + grp * NG) * TOK + cs
                    ctok = DMA("gpsimd", crs[:].rearrange("p (a b) -> p a b", a=NG),
                               AP(cc2_out.tensor, soff, [[0, 128], [TOK, NG], [1, 512]]), inc="srep")
                    das = dA_rb.acq("scalar")
                    W("scalar", dltok)
                    datok = None
                    for i in range(NG):
                        datok = OP("scalar", lambda e, d=dl, o=das, i=i, n0=n0: e.activation(
                            o[:, i * 512:(i + 1) * 512], d[:], ACT.Exp, scale=s_aneg[:, n0 + i:n0 + i + 1]), inc="sda")
                    gs_ = g_rb.acq("vector")
                    W("vector", qtok)
                    W("vector", btok)
                    qv = AP(q_st.tensor, q_st.offset, [[q_st.ap[0][0], 128], [0, NG], [1, 512]])
                    gtok = OP("vector", lambda e, o=gs_, qv=qv, b=brs: e.tensor_tensor(
                        o[:].rearrange("p (a b) -> p a b", a=NG), qv,
                        b[:].rearrange("p (a b) -> p a b", a=NG), ALU.mult), inc="svg")
                    br_rb.ret(gtok)
                    W("vector", datok)
                    sctok = None
                    for i in range(NG):
                        def scfn(e, i=i, das=das, gs_=gs_, n0=n0):
                            return e.tensor_tensor_scan(
                                h_b[:, i * 512:(i + 1) * 512], das[:, i * 512:(i + 1) * 512],
                                gs_[:, i * 512:(i + 1) * 512], s_carry[:, n0 + i:n0 + i + 1],
                                ALU.mult, ALU.add)
                        sctok = OP("vector", scfn, inc="svs")
                    dA_rb.ret(sctok)
                    g_rb.ret(sctok)
                    hlast = AP(h_b.tensor, h_b.offset + 511, [[h_b.ap[0][0], 128], [512, NG]])
                    OP("vector", lambda e, n0=n0, hl=hlast: e.tensor_copy(s_carry[:, n0:n0 + NG], hl))
                    W("vector", ctok)
                    cttok = OP("vector", lambda e, c_=crs: e.tensor_tensor(
                        hc_b[:].rearrange("p (a b) -> p a b", a=NG),
                        h_b[:].rearrange("p (a b) -> p a b", a=NG),
                        c_[:].rearrange("p (a b) -> p a b", a=NG), ALU.mult), inc="svh")
                    cr_rb.ret(cttok)
                    hcv = AP(hc_b.tensor, hc_b.offset, [[hc_b.ap[0][0], 128], [1, 512], [512, NG]])
                    if grp == 0:
                        OP("vector", lambda e, hv=hcv: e.tensor_reduce(yac[:], hv, AXL.X, ALU.add))
                    else:
                        OP("vector", lambda e, hv=hcv: e.tensor_reduce(ytmp[:], hv, AXL.X, ALU.add))
                        OP("vector", lambda e: e.tensor_tensor(yac[:], ytmp[:], yac[:], ALU.add))
                OP("vector", lambda e, g=g: e.tensor_copy(yst[:, g * 512:(g + 1) * 512], yac[:]), inc="svy")
            OP("vector", lambda e: e.memset(ytmp[:], 0.0), inc="svy")
            dt_rb.ret(dmm_tok)
            W("gpsimd", "svy")
            ydst = AP(y_all.tensor, c * 512, [[T_, 128], [T_ * 128, 8], [1, 512]])
            ydtok = DMA("gpsimd", ydst, yst[:].rearrange("p (g t) -> p g t", g=8), inc="syd")
            y_prev_dma[0] = ydtok

        fcbank = BankRB([0, 1])
        pjbank = BankRB([2, 3])
        hact_guard = [None]

        def emit_slot(s):
            shared = (s == 0)
            e_ = s - 1
            wfc_src = wfc_s if shared else wfc_in[e_]
            wpr_src = wproj_s if shared else wproj_in[e_]
            for ch in range(4):
                cs = ch * 512
                xs = xn_rb.acq("gpsimd")
                if s == 0 and ch == 0:
                    W("gpsimd", ("xnd", CNT["xnd"]))
                    W("gpsimd", tg_all)
                xtok = DMA("gpsimd", v3(xs[:], 8), xn_dram[:, cs:cs + 512].rearrange("(kt p) t -> p kt t", p=128), inc="cxn")
                grtok = None
                grs = None
                if not shared:
                    grs = gr_rb.acq("gpsimd")
                    if s == 1 and ch == 0:
                        W("gpsimd", ("gd", CNT["gd"]))
                    grtok = DMA("gpsimd", grs[:], AP(gate_dram.tensor, e_ * TOK + cs, [[0, 128], [1, 512]]), inc="cgr")
                    if DEBUG and s == 1 and ch == 0:
                        W("gpsimd", grtok)
                        dgtok = DMA("gpsimd", dbg["dbg_grep"], grs[:], inc="dbgs")
                        W("gpsimd", dgtok)
                W("tensor", xtok)
                fctok = None
                for mt in range(32):
                    wfs = wfc_rb.acq("gpsimd")
                    wtok = DMA("gpsimd", v3(wfs[:], 8), wfc_src[mt], inc="cwf")
                    if DEBUG and s == 0 and ch == 0 and mt == 20:
                        W("gpsimd", wtok)
                        dwtok = DMA("gpsimd", dbg["dbg_wfc"], wfs[:], inc="dbgs")
                        W("tensor", dwtok)
                    bk = fcbank.acq("tensor")
                    bank = pb[bk]
                    W("tensor", wtok)
                    for kt in range(8):
                        def mmfn(e, b=bank, w=wfs, kt=kt, x=xs):
                            return e.matmul(b[:], w[:, kt * 128:(kt + 1) * 128],
                                            x[:, kt * 512:(kt + 1) * 512],
                                            start=(kt == 0), stop=(kt == 7))
                        fctok = OP("tensor", mmfn, inc=("fcm" if kt == 7 else None))
                    wfc_rb.ret(fctok)
                    bias = s_bfcs[:, mt:mt + 1] if shared else s_bfc[:, e_ * 32 + mt: e_ * 32 + mt + 1]
                    W("scalar", fctok)
                    if hact_guard[0] is not None:
                        W("scalar", hact_guard[0])
                    getok = OP("scalar", lambda e, b=bank, mt=mt, bias=bias: e.activation(
                        hact[:, mt * 512:(mt + 1) * 512], b[:], ACT.Gelu_apprx_tanh, bias=bias), inc="fce")
                    fcbank.ret(getok)
                xn_rb.ret(fctok)
                fce_all = ("fce", CNT["fce"])
                if DEBUG and s == 0 and ch == 0:
                    W("gpsimd", fctok)
                    DMA("gpsimd", dbg["dbg_xnst"], xs[:], inc="dbgs")
                    W("gpsimd", ("fce", CNT["fce"]))
                    dhtok = DMA("gpsimd", dbg["dbg_hact"], hact[:], inc="dbgs")
                    W("tensor", dhtok)
                W("tensor", fce_all)
                ptok = None
                vtok = None
                for mo in range(8):
                    wps = wpr_rb.acq("gpsimd")
                    wtok = DMA("gpsimd", v3(wps[:], 32), wpr_src[mo], inc="cwp")
                    bk = pjbank.acq("tensor")
                    bank = pb[bk]
                    W("tensor", wtok)
                    for kt in range(32):
                        stop = (kt == 31) and (not shared)
                        def mmfn(e, b=bank, w=wps, kt=kt, stop=stop):
                            return e.matmul(b[:], w[:, kt * 128:(kt + 1) * 128],
                                            hact[:, kt * 512:(kt + 1) * 512],
                                            start=(kt == 0), stop=stop)
                        ptok = OP("tensor", mmfn, inc=("pjm" if stop else None))
                    if shared:
                        if mo == 0 and ch == 0:
                            W("tensor", gfm_all)
                        def gbfn(e, b=bank, mo=mo, cs=cs):
                            return e.matmul(b[:], s_bproj[0:8, mo * 128:(mo + 1) * 128],
                                            s_gateb[0:8, cs:cs + 512], start=False, stop=True)
                        ptok = OP("tensor", gbfn, inc="pjm")
                    wpr_rb.ret(ptok)
                    accs = acc_sb[:, mo * TOK + cs: mo * TOK + cs + 512]
                    W("vector", ptok)
                    if shared:
                        vtok = OP("vector", lambda e, b=bank, o=accs, mo=mo: e.tensor_scalar(
                            o, b[:], s_bprs[:, mo:mo + 1], None, ALU.add), inc="pje")
                    else:
                        W("vector", grtok)
                        OP("vector", lambda e, b=bank, g=grs: e.tensor_tensor(
                            tmp_ev[:], b[:], g[:], ALU.mult))
                        vtok = OP("vector", lambda e, o=accs: e.tensor_tensor(
                            o, tmp_ev[:], o, ALU.add), inc="pje")
                    pjbank.ret(vtok)
                if not shared:
                    gr_rb.ret(vtok)
                hact_guard[0] = ("pjm", CNT["pjm"])

        def emit_slot9(cc3tok):
            for ch in range(4):
                cs = ch * 512
                W("gpsimd", cc3tok)
                if ch == 0:
                    W("gpsimd", ("zd", CNT["zd"]))
                    W("gpsimd", ("pjm", CNT["pjm"]))
                    W("gpsimd", ("pje", CNT["pje"]))
                if ch > 0:
                    W("gpsimd", ("yfv", CNT["yfv"]))
                if ch > 0:
                    W("vector", ("wom", CNT["wom"]))
                wotok = None
                for dt in range(16):
                    sl = slice(dt * 512, (dt + 1) * 512)
                    uds = ud_rb.acq("gpsimd")
                    uctok = DMA("gpsimd", uds[:], uc_dram[dt * 128:(dt + 1) * 128, cs:cs + 512], inc="c1u")
                    zds = zd_rb.acq("gpsimd")
                    ztok = DMA("gpsimd", zds[:], z_dram[dt * 128:(dt + 1) * 128, cs:cs + 512], inc="c1z")
                    r0 = dt * 128 if dt < 8 else SD + (dt - 8) * 128
                    ypair = yp_rb.acq("gpsimd")
                    DMA("gpsimd", ypair[:, 0:512], AP(cc3_out.tensor, r0 * T_ + cs, [[T_, 128], [1, 512]]), inc="c1y")
                    r1 = SD + dt * 128 if dt < 8 else (dt - 8) * 128
                    ytok = DMA("gpsimd", ypair[:, 512:1024], AP(cc3_out.tensor, r1 * T_ + TOK + cs, [[T_, 128], [1, 512]]), inc="c1y")
                    W("vector", uctok)
                    W("vector", ytok)
                    OP("vector", lambda e, p=ypair: e.tensor_scalar(ybl[:], p[:, 0:512], s_hmask[:, 0:1], None, ALU.mult))
                    yblt = OP("vector", lambda e, p=ypair: e.scalar_tensor_tensor(ybl[:], p[:, 512:1024], s_hmask[:, 1:2], ybl[:], ALU.mult, ALU.add), inc="ybl")
                    yp_rb.ret(yblt)
                    y1t = OP("vector", lambda e, u=uds, dt=dt: e.scalar_tensor_tensor(
                        y1_sc[:], u[:], s_dvec[:, dt:dt + 1], ybl[:], ALU.mult, ALU.add), inc="ybl")
                    ud_rb.ret(y1t)
                    W("scalar", ztok)
                    thtok = OP("scalar", lambda e, z=zds: e.activation(th_sc[:], z[:], ACT.Tanh, scale=s_cst[:, 0:1]), inc="c1t")
                    W("vector", thtok)
                    OP("vector", lambda e: e.tensor_scalar(th_sc[:], th_sc[:], 0.5, 0.5, ALU.mult, ALU.add))
                    svt = OP("vector", lambda e, z=zds: e.tensor_tensor(sv_sc[:], th_sc[:], z[:], ALU.mult), inc="ybl")
                    zd_rb.ret(svt)
                    OP("vector", lambda e, sl=sl: e.tensor_tensor(yf_ch[:, sl], y1_sc[:], sv_sc[:], ALU.mult), inc="yfv")
                yfv_all = ("yfv", CNT["yfv"])
                for mo in range(8):
                    wos = wo_rb.acq("gpsimd")
                    wtok = DMA("gpsimd", v3(wos[:], 16), w_outT[mo], inc="cwo")
                    bk = pjbank.acq("tensor")
                    bank = pb[bk]
                    W("tensor", wtok)
                    W("tensor", yfv_all)
                    wotok = None
                    for kt in range(16):
                        def mmfn(e, b=bank, w=wos, kt=kt):
                            return e.matmul(b[:], w[:, kt * 128:(kt + 1) * 128],
                                            yf_ch[:, kt * 512:(kt + 1) * 512],
                                            start=(kt == 0), stop=(kt == 15))
                        wotok = OP("tensor", mmfn, inc=("wom" if kt == 15 else None))
                    wo_rb.ret(wotok)
                    accs = acc_sb[:, mo * TOK + cs: mo * TOK + cs + 512]
                    W("vector", wotok)
                    vtok = OP("vector", lambda e, b=bank, o=accs: e.tensor_tensor(
                        o, b[:], o, ALU.add), inc="woe")
                    pjbank.ret(vtok)

        def emit_c4():
            acc_done = ("woe", CNT["woe"])
            pje_done = ("pje", CNT["pje"])
            tbank = BankRB([4, 5])
            W("sync", acc_done)
            for ch in range(4):
                cs = ch * 512
                for tt in range(4):
                    xr = xr_rb.acq("sync")
                    xrtok = DMA("sync", xr[:], x_in[1 + cs + tt * 128: 1 + cs + (tt + 1) * 128, :], inc="c4x")
                    ot = ot_rb.acq("vector")
                    if ch == 0 and tt == 0:
                        W("tensor", acc_done)
                        W("tensor", pje_done)
                    trtok = None
                    vtok = None
                    for cb in range(8):
                        bk = tbank.acq("tensor")
                        bank = pb[bk]
                        def trfn(e, b=bank, cb=cb, tt=tt, cs=cs):
                            return e.transpose(b[:, 0:128], acc_sb[:, cb * TOK + cs + tt * 128: cb * TOK + cs + (tt + 1) * 128], s_ident[:])
                        trtok = OP("tensor", trfn, inc="c4t")
                        W("vector", trtok)
                        W("vector", xrtok)
                        vtok = OP("vector", lambda e, b=bank, o=ot, cb=cb, x=xr: e.scalar_tensor_tensor(
                            o[:, cb * 128:(cb + 1) * 128], b[:, 0:128], 1.0, x[:, cb * 128:(cb + 1) * 128],
                            ALU.mult, ALU.add), inc="c4v")
                        tbank.ret(vtok)
                    xr_rb.ret(vtok)
                    W("sync", vtok)
                    ottok = DMA("sync", out_d[cs + tt * 128: cs + (tt + 1) * 128, :], ot[:], inc="outd")
                    ot_rb.ret(ottok)

        import os
        STAGE = os.environ.get("KSTAGE", "full")
        if STAGE in ("B", "C", "full"):
            for c in range(SCH):
                emit_scan_chunk(c)
            for s_ in ("syd", "sdt", "sug", "srep", "gq"):
                if CNT.get(s_, 0) > 0:
                    W("gpsimd", (s_, CNT[s_]))
            OP("gpsimd", lambda e: e.collective_compute(
                "AllGather", ALU.bypass, replica_groups=RG_PAIRS,
                ins=[y_all.opt()], outs=[cc3_out.opt()]), inc="cc3")
            cc3tok = ("cc3", CNT["cc3"])
            W("gpsimd", cc3tok)
        if STAGE in ("C", "full"):
            for s in range(9):
                emit_slot(s)
        elif STAGE.startswith("S"):
            for s in range(int(STAGE[1:])):
                emit_slot(s)
        if STAGE == "full":
            emit_slot9(cc3tok)
            emit_c4()
        else:
            DMA("gpsimd", out_d, x_in[1:1 + TOK, :], inc="outd")

        if DEBUG:
            W("gpsimd", ("xnd", CNT["xnd"]))
            W("gpsimd", ("ucd", CNT["ucd"]))
            W("gpsimd", ("zd", CNT["zd"]))
            W("gpsimd", cc2tok)
            W("gpsimd", ("gd", CNT["gd"]))
            W("gpsimd", cc3tok)
            DMA("gpsimd", dbg["dbg_xn"], xn_dram, inc="dbgd")
            DMA("gpsimd", dbg["dbg_uc"], uc_dram, inc="dbgd")
            DMA("gpsimd", dbg["dbg_z"], z_dram, inc="dbgd")
            DMA("gpsimd", dbg["dbg_bcdt"], cc2_out, inc="dbgd")
            DMA("gpsimd", dbg["dbg_gate"], gate_dram, inc="dbgd")
            DMA("gpsimd", dbg["dbg_yall"], cc3_out, inc="dbgd")
            W("gpsimd", ("woe", CNT["woe"]))
            W("gpsimd", ("pje", CNT["pje"]))
            DMA("gpsimd", dbg["dbg_acc"].rearrange("(mo p) t -> p mo t", p=128), v3(acc_sb[:], 8), inc="dbgd")
            DMA("gpsimd", dbg["dbg_lgt"], lgt_sb, inc="dbgd")
            DMA("gpsimd", dbg["dbg_gtok"], gt_sb, inc="dbgd")
            DMA("gpsimd", dbg["dbg_gfm"], s_gate[0:8, :], inc="dbgd")
            W("gpsimd", ("dbgd", CNT["dbgd"]))

        W("gpsimd", ("outd", CNT["outd"]))
        W("sync", ("outd", CNT["outd"]))

        with nc.Block() as block:
            @block.sync
            def _(e):
                for f in SQ["sync"]:
                    f(e)

            @block.tensor
            def _(e):
                for f in SQ["tensor"]:
                    f(e)

            @block.vector
            def _(e):
                for f in SQ["vector"]:
                    f(e)

            @block.scalar
            def _(e):
                for f in SQ["scalar"]:
                    f(e)

            @block.gpsimd
            def _(e):
                for f in SQ["gpsimd"]:
                    f(e)

    return nc


def _prep_core_inputs(c, inputs):
    h = c % 2
    b = c // 2
    t0 = h * TOK
    f32 = np.float32

    x = np.asarray(inputs["L0_attn_x"], f32)
    W_in = np.asarray(inputs["L0_attn_W_in"], f32)
    conv_w = np.asarray(inputs["L0_attn_conv_w"], f32)
    conv_b = np.asarray(inputs["L0_attn_conv_b"], f32)
    W_xproj = np.asarray(inputs["L0_attn_W_xproj"], f32)
    W_dt = np.asarray(inputs["L0_attn_W_dt"], f32)
    b_dt = np.asarray(inputs["L0_attn_b_dt"], f32)
    A_log = np.asarray(inputs["L0_attn_A_log"], f32)
    Dv = np.asarray(inputs["L0_attn_D"], f32)
    W_out = np.asarray(inputs["L0_attn_W_out"], f32)
    Wfc_s = np.asarray(inputs["L0_ffn_Wfc_s"], f32)
    bfc_s = np.asarray(inputs["L0_ffn_bfc_s"], f32)
    Wproj_s = np.asarray(inputs["L0_ffn_Wproj_s"], f32)
    bproj_s = np.asarray(inputs["L0_ffn_bproj_s"], f32)
    Wr = np.asarray(inputs["L0_ffn_Wr"], f32)
    Wfc = np.asarray(inputs["L0_ffn_Wfc"], f32)
    bfc = np.asarray(inputs["L0_ffn_bfc"], f32)
    Wproj = np.asarray(inputs["L0_ffn_Wproj"], f32)
    bproj = np.asarray(inputs["L0_ffn_bproj"], f32)

    perm = np.concatenate([np.arange(h * SD, (h + 1) * SD),
                           np.arange((1 - h) * SD, (2 - h) * SD)])

    def pack16(v):
        return np.ascontiguousarray(v[perm].reshape(16, 128).T)

    x_pad = np.zeros((XROWS, D_), f32)
    lo = max(t0 - 1, 0)
    hi = min(t0 + TOK + 1, T_)
    x_pad[lo - (t0 - 1): lo - (t0 - 1) + hi - lo] = x[b, lo:hi]

    W_inT = W_in.T
    w_inT_c = W_inT[:, np.concatenate([perm, DE + perm])]
    w_inT_c = np.ascontiguousarray(
        w_inT_c.reshape(8, 128, 32, 128).transpose(2, 1, 0, 3))

    cw = conv_w[:, 0, :][perm]
    convw_c = np.ascontiguousarray(cw.reshape(16, 128, 3).transpose(1, 0, 2).reshape(128, 48))

    a_sl = A_log[perm[:SD]]
    a_pack = np.ascontiguousarray(a_sl.reshape(8, 128, 16).transpose(1, 0, 2).reshape(128, 128))

    bfc_pack = np.ascontiguousarray(bfc.reshape(NEXP, 32, 128).transpose(2, 0, 1).reshape(128, NEXP * 32))

    return {
        "x_in": x_pad,
        "w_inT": w_inT_c,
        "convw": convw_c,
        "convb": pack16(conv_b),
        "dvec": pack16(Dv),
        "w_xprojT": np.ascontiguousarray(W_xproj.T[perm]),
        "w_dtT": np.ascontiguousarray(W_dt.T[:, perm[:SD]]),
        "b_dt": np.ascontiguousarray(b_dt[perm[:SD]].reshape(8, 128).T),
        "a_pack": a_pack,
        "w_outT": np.ascontiguousarray(W_out.T[perm].reshape(16, 128, 8, 128).transpose(2, 1, 0, 3)),
        "wfc_s": np.ascontiguousarray(Wfc_s.reshape(8, 128, 32, 128).transpose(2, 1, 0, 3)),
        "bfc_s": np.ascontiguousarray(bfc_s.reshape(32, 128).T),
        "wproj_s": np.ascontiguousarray(Wproj_s.reshape(32, 128, 8, 128).transpose(2, 1, 0, 3)),
        "bproj_s": np.ascontiguousarray(bproj_s.reshape(8, 128).T),
        "wr": np.ascontiguousarray(Wr),
        "wfc": np.ascontiguousarray(Wfc.reshape(8, 8, 128, 32, 128).transpose(0, 3, 2, 1, 4)),
        "bfc": bfc_pack,
        "wproj": np.ascontiguousarray(Wproj.reshape(8, 32, 128, 8, 128).transpose(0, 3, 2, 1, 4)),
        "bproj": np.ascontiguousarray(bproj),
        "ident": np.eye(128, dtype=f32),
        "hmask": np.ascontiguousarray(np.broadcast_to(np.array([1.0 - h, float(h)], f32), (128, 2))),
    }


def kernel(**inputs):
    if "nc" not in _BUILT:
        _BUILT["nc"] = build_nc()
    nc = _BUILT["nc"]
    in_maps = [_prep_core_inputs(c, inputs) for c in range(8)]
    res = run_bass_kernel_spmd(nc, in_maps, core_ids=list(range(8)))
    out = np.empty((B_, T_, D_), np.float32)
    for c in range(8):
        h, b = c % 2, c // 2
        out[b, h * TOK:(h + 1) * TOK, :] = res.results[c]["out"]
    return out
